# version 07
# baseline (speedup 1.0000x reference)
import numpy as np
import ml_dtypes
from contextlib import ExitStack

import concourse.bass as bass
import concourse.mybir as mybir
from concourse.bass_utils import run_bass_kernel_spmd

F32 = mybir.dt.float32
BF16 = mybir.dt.bfloat16
I32 = mybir.dt.int32
U32 = mybir.dt.uint32
AF = mybir.ActivationFunctionType
ALU = mybir.AluOpType
AX = mybir.AxisListType

P = 128
D = 4096
NCORES = 8
TOK = 1024
NTC = 4
KSL = 32
NCH = 8
EPS = 1e-5
INV_D = 1.0 / D

LN_PH = {"L1A": 0, "L1B": 1, "L2A": 2, "L3A": 3, "L2B": 4, "L3B": 5}
FFN_PH = {"F1A": 0, "F1B": 1, "F2A": 2, "F2B": 3}
LN_CHUNK = {"L1A": 0, "L1B": 1, "L2A": 0, "L3A": 0, "L2B": 1, "L3B": 1}
FFN_CHUNK = {"F1A": 0, "F1B": 1, "F2A": 0, "F2B": 1}
FFN_BANK_BASE = {"F1A": 0, "F1B": 8, "F2A": 24, "F2B": 40}
FFN_SQPRE = {"F1A": 4, "F1B": 8, "F2A": 16, "F2B": 24}
ATTN_BANK_BASE = {0: 16, 1: 32}


DBG = False


def build_nc():
    nc = bass.Bass("TRN2", target_bir_lowering=False, detect_race_conditions=False)

    xin = nc.dram_tensor("xin", [TOK, D], F32, kind="ExternalInput").ap()
    maskT = nc.dram_tensor("maskT", [512, D], I32, kind="ExternalInput").ap()
    wT = nc.dram_tensor("wT", [512, D], F32, kind="ExternalInput").ap()
    bias_h = nc.dram_tensor("bias_h", [P, D], F32, kind="ExternalInput").ap()
    rbb_h = nc.dram_tensor("rbb_h", [P, 8], F32, kind="ExternalInput").ap()
    rwT_h = nc.dram_tensor("rwT_h", [D, 8], F32, kind="ExternalInput").ap()
    aT_h = nc.dram_tensor("aT_h", [D, P], F32, kind="ExternalInput").ap()
    bT_h = nc.dram_tensor("bT_h", [P, D], F32, kind="ExternalInput").ap()
    out_h = nc.dram_tensor("out_h", [TOK, D], F32, kind="ExternalOutput").ap()
    if DBG:
        dbg_st = nc.dram_tensor("dbg_st", [P, 64], F32, kind="ExternalOutput").ap()
        dbg_xn = nc.dram_tensor("dbg_xn", [P, 512], BF16, kind="ExternalOutput").ap()
        dbg_lg = nc.dram_tensor("dbg_lg", [P, 64], F32, kind="ExternalOutput").ap()
        dbg_xnT = nc.dram_tensor("dbg_xnT", [P, KSL * 512], BF16, kind="ExternalOutput").ap()
        dbg_wq0 = nc.dram_tensor("dbg_wq0", [P, 1024], BF16, kind="ExternalOutput").ap()
        dbg_stage = nc.dram_tensor("dbg_stage", [P, 2048], BF16, kind="ExternalOutput").ap()
        dbg_gc = nc.dram_tensor("dbg_gc", [P, 40], F32, kind="ExternalOutput").ap()
        dbg_wt = nc.dram_tensor("dbg_wt", [P, 512], BF16, kind="ExternalOutput").ap()
        dbg_bm = nc.dram_tensor("dbg_bm", [P, 48], F32, kind="ExternalOutput").ap()

    warm_in = nc.dram_tensor("warm_in", [P, 8], F32, kind="Internal")
    warm_out = nc.dram_tensor("warm_out", [NCORES * P, 8], F32, kind="Internal", addr_space="Shared")
    ag_in = [nc.dram_tensor(f"ag_in{j}", [P, 2048], BF16, kind="Internal") for j in range(NCH)]
    ag_out = [nc.dram_tensor(f"ag_out{j}", [NCORES * P, 2048], BF16, kind="Internal", addr_space="Shared")
              for j in range(NCH)]

    with ExitStack() as ctx:
        e = ctx.enter_context

        x = e(nc.sbuf_tensor("x", [P, 8 * D], F32))
        xnT = e(nc.sbuf_tensor("xnT", [P, KSL * 512], BF16))
        AT = e(nc.sbuf_tensor("AT", [P, KSL * P], BF16))
        BT = e(nc.sbuf_tensor("BT", [P, D], BF16))
        bias_bc = e(nc.sbuf_tensor("bias_bc", [P, D], BF16))
        rwT = e(nc.sbuf_tensor("rwT", [P, KSL * 8], BF16))
        rb_bc = e(nc.sbuf_tensor("rb_bc", [P, 8], F32))
        ident = e(nc.sbuf_tensor("ident", [P, P], BF16))
        ind32 = e(nc.sbuf_tensor("ind32", [P, 4], BF16))
        ind4 = e(nc.sbuf_tensor("ind4", [4, P], BF16))
        ind16 = e(nc.sbuf_tensor("ind16", [8, P], BF16))
        iota8 = e(nc.sbuf_tensor("iota8", [P, 8], F32))
        epst = e(nc.sbuf_tensor("epst", [P, 1], F32))
        dmy = e(nc.sbuf_tensor("dmy", [P, 16], F32))
        dmy2 = e(nc.sbuf_tensor("dmy2", [P, 16], F32))
        gcap = e(nc.sbuf_tensor("gcap", [P, 40], F32))
        sdmya = e(nc.sbuf_tensor("sdmya", [P, 1], F32))
        wqbuf = [e(nc.sbuf_tensor(f"wqbuf{j}", [P, 1024], BF16)) for j in range(2)]
        xstg = [e(nc.sbuf_tensor(f"xstg{j}", [P, 512], BF16)) for j in range(2)]
        hw = e(nc.sbuf_tensor("hw", [P, 512], BF16))
        wrs = e(nc.sbuf_tensor("wrs", [P, 512], BF16))
        w8 = e(nc.sbuf_tensor("w8", [8, 512], BF16))
        m8 = e(nc.sbuf_tensor("m8", [P, 8], F32))
        lg = [e(nc.sbuf_tensor(f"lg_{t}", [P, 8], F32)) for t in range(4)]
        lg2 = [e(nc.sbuf_tensor(f"lg2_{t}", [P, 8], F32)) for t in range(4)]
        f1a = [e(nc.sbuf_tensor(f"f1a{t}", [P, 8], F32)) for t in range(4)]
        f1b = [e(nc.sbuf_tensor(f"f1b{t}", [P, 8], F32)) for t in range(4)]
        wq = [e(nc.sbuf_tensor(f"wq{t}", [P, 8], BF16)) for t in range(4)]
        g_m1 = [e(nc.sbuf_tensor(f"g_m1{t}", [P, 1], F32)) for t in range(4)]
        g_m2 = [e(nc.sbuf_tensor(f"g_m2{t}", [P, 1], F32)) for t in range(4)]
        g_d = [e(nc.sbuf_tensor(f"g_d{t}", [P, 1], F32)) for t in range(4)]
        g_e = [e(nc.sbuf_tensor(f"g_e{t}", [P, 1], F32)) for t in range(4)]
        g_den = [e(nc.sbuf_tensor(f"g_den{t}", [P, 1], F32)) for t in range(4)]
        g_r = [e(nc.sbuf_tensor(f"g_r{t}", [P, 1], F32)) for t in range(4)]
        g_1 = [e(nc.sbuf_tensor(f"g_1{t}", [P, 1], F32)) for t in range(4)]
        g_2 = [e(nc.sbuf_tensor(f"g_2{t}", [P, 1], F32)) for t in range(4)]
        bnb = e(nc.sbuf_tensor("bnb", [P, 48], F32))
        magg = [e(nc.sbuf_tensor(f"magg{j}", [P, 2], F32)) for j in range(2)]
        st_sd = [e(nc.sbuf_tensor(f"st_sd{j}", [P, 1], F32)) for j in range(2)]
        st_r = [e(nc.sbuf_tensor(f"st_r{j}", [P, 1], F32)) for j in range(2)]
        st_b = [e(nc.sbuf_tensor(f"st_b{j}", [P, 1], F32)) for j in range(2)]
        mtile = [e(nc.sbuf_tensor(f"mtile{j}", [P, 512], I32)) for j in range(2)]
        wtile = [e(nc.sbuf_tensor(f"wtile{j}", [P, 512], BF16)) for j in range(2)]
        rsb = e(nc.sbuf_tensor("rsb", [P, 16], BF16))
        bm01 = e(nc.sbuf_tensor("bm01", [4, 16], BF16))
        bmPs = e(nc.sbuf_tensor("bmPs", [P, 16], F32))
        stage = e(nc.sbuf_tensor("stage", [P, 2048], BF16))

        pa = [e(nc.psum_tensor(f"pa{t}", [P, 512], F32)) for t in range(4)]
        ptr = [e(nc.psum_tensor(f"ptr{j}", [P, 512], BF16)) for j in range(2)]
        p6 = e(nc.psum_tensor("p6", [P, 512], F32))
        p7 = e(nc.psum_tensor("p7", [P, 512], F32))

        def sem(name):
            return e(nc.semaphore(name))

        sxA, sxB = sem("sxA"), sem("sxB")
        sdbg = sem("sdbg")
        swrm = sem("swrm")
        swrmd = sem("swrmd")
        scon = sem("scon")
        swAT, swBT, swRW, swBI, swRB = sem("swAT"), sem("swBT"), sem("swRW"), sem("swBI"), sem("swRB")
        sag = sem("sag")
        spm = [sem("spm0"), sem("spm1")]
        spw = [sem("spw0"), sem("spw1")]
        spsV = sem("spsV")
        spsT = sem("spsT")
        spe = sem("spe")
        spd = sem("spd")
        swq = [sem("swq0"), sem("swq1")]
        swf = sem("swf")
        sbank = [sem(f"sbank{t}") for t in range(4)]
        sqd = sem("sqd")
        svd = sem("svd")
        ssd = sem("ssd")
        srb = sem("srb")
        snm = sem("snm")
        strd = sem("strd")
        sted = sem("sted")
        sbia = sem("sbia")
        srld = sem("srld")
        slgd = sem("slgd")
        sgex = sem("sgex")
        sgeb = sem("sgeb")
        swqd = sem("swqd")
        swqt = sem("swqt")
        sw8d = sem("sw8d")
        shd = sem("shd")
        swrd = sem("swrd")
        swre = sem("swre")
        shwd = sem("shwd")
        sm2s = sem("sm2s")
        souts = sem("souts")

        block = e(nc.Block())

        def xcol(t_abs, off, width):
            return x[:, t_abs * D + off: t_abs * D + off + width]

        LN_ORDER = ["L1A", "L1B", "L2A", "L3A", "L2B", "L3B"]
        STED_END = {ph: (LN_PH[ph] + 1) * 32 for ph in LN_ORDER}

        LN_XREADY = {"L1A": None, "L1B": None, "L2A": 8, "L3A": 24, "L2B": 16, "L3B": 40}
        LN_HWPRE = {"L1A": 0, "L1B": 1, "L2A": 2, "L3A": 2, "L2B": 3, "L3B": 3}

        @block.sync
        def _(sync):
            for c, s_ in ((0, sxA), (1, sxB)):
                src = xin.rearrange("(tl p) d -> tl p d", p=P)[c * NTC:(c + 1) * NTC]
                dst = x[:].rearrange("p (tl d) -> p tl d", d=D)[:, c * NTC:(c + 1) * NTC]
                sync.dma_start(out=dst, in_=src.rearrange("tl p d -> p tl d")).then_inc(s_, 16)

            for rt in range(KSL):
                j2 = rt % 2
                if rt >= 2:
                    sync.wait_ge(spe, rt - 1)
                r0 = (rt % 4) * P
                c0 = (rt // 4) * 512
                sync.dma_start(out=mtile[j2][:], in_=maskT[r0:r0 + P, c0:c0 + 512]).then_inc(spm[j2], 16)

            for c in range(2):
                for nch in range(NCH):
                    if c == 0:
                        sync.wait_ge(sag, nch + 2)
                    for q in range(16):
                        Q = c * 128 + nch * 16 + q
                        j2 = Q % 2
                        if Q >= 2:
                            sync.wait_ge(swf, Q - 1)
                        cb, hf = q // 2, q % 2
                        sync.dma_start(out=wqbuf[j2][:],
                                       in_=ag_out[nch].ap()[cb * P:(cb + 1) * P,
                                                            hf * 1024:(hf + 1) * 1024]).then_inc(swq[j2], 16)
                        if DBG and Q == 0:
                            sync.wait_ge(swq[0], 16)
                            sync.dma_start(out=dbg_wq0, in_=wqbuf[0][:]).then_inc(sdbg, 16)
                if c == 0:
                    for t in range(4):
                        sync.wait_ge(sbank[t], 32)
                    for i in range(NTC):
                        sync.dma_start(out=out_h[i * P:(i + 1) * P, :],
                                       in_=xcol(i, 0, D)).then_inc(souts, 16)
            for t in range(4):
                sync.wait_ge(sbank[t], 48)
            for i in range(NTC):
                t_abs = NTC + i
                sync.dma_start(out=out_h[t_abs * P:(t_abs + 1) * P, :],
                               in_=xcol(t_abs, 0, D)).then_inc(souts, 16)
            if DBG:
                sync.wait_ge(sdbg, 16 * 13)
            sync.wait_ge(souts, 16 * 8)

        @block.gpsimd
        def _(g):
            g.memset(m8[:], 0.0).then_inc(swrm, 1)
            g.wait_ge(swrm, 1)
            g.dma_start(out=warm_in.ap(), in_=m8[:]).then_inc(swrmd, 16)
            g.wait_ge(swrmd, 16)
            g.collective_compute("AllGather", ALU.bypass, replica_groups=[list(range(NCORES))],
                                 ins=[warm_in.ap().opt()], outs=[warm_out.ap().opt()]).then_inc(sag, 1)

            cnt = 0

            def step(ins, wait=True):
                nonlocal cnt
                cnt += 1
                ins.then_inc(scon, 1)
                if wait:
                    g.wait_ge(scon, cnt)
                return cnt

            step(g.memset(ident[:], 0.0))
            step(g.affine_select(out=ident[:], in_=ident[:], compare_op=ALU.not_equal,
                                 fill=1.0, base=0, pattern=[[-1, P]], channel_multiplier=1))
            step(g.memset(ind32[:], 1.0))
            step(g.affine_select(out=ind32[:], in_=ind32[:], compare_op=ALU.is_ge,
                                 fill=0.0, base=0, pattern=[[-32, 4]], channel_multiplier=1))
            step(g.affine_select(out=ind32[:], in_=ind32[:], compare_op=ALU.is_ge,
                                 fill=0.0, base=31, pattern=[[32, 4]], channel_multiplier=-1))
            step(g.memset(ind4[:], 1.0))
            step(g.affine_select(out=ind4[:], in_=ind4[:], compare_op=ALU.is_ge,
                                 fill=0.0, base=0, pattern=[[1, P]], channel_multiplier=-32))
            step(g.affine_select(out=ind4[:], in_=ind4[:], compare_op=ALU.is_ge,
                                 fill=0.0, base=31, pattern=[[-1, P]], channel_multiplier=32))
            step(g.memset(ind16[:], 1.0))
            step(g.affine_select(out=ind16[:], in_=ind16[:], compare_op=ALU.is_ge,
                                 fill=0.0, base=0, pattern=[[1, P]], channel_multiplier=-16))
            step(g.affine_select(out=ind16[:], in_=ind16[:], compare_op=ALU.is_ge,
                                 fill=0.0, base=15, pattern=[[-1, P]], channel_multiplier=16))
            step(g.memset(epst[:], EPS))
            step(g.iota(iota8[:], pattern=[[1, 8]], base=0, channel_multiplier=0,
                        allow_small_or_imprecise_dtypes=True))
            CONSTS_DONE = cnt

            g.dma_start(out=AT[:].rearrange("p (kk er) -> p kk er", er=P),
                        in_=aT_h.rearrange("(kk p) er -> p kk er", p=P)).then_inc(swAT, 16)
            g.dma_start(out=BT[:], in_=bT_h).then_inc(swBT, 16)
            g.dma_start(out=rwT[:].rearrange("p (kk e) -> p kk e", e=8),
                        in_=rwT_h.rearrange("(kk p) e -> p kk e", p=P)).then_inc(swRW, 16)
            g.dma_start(out=bias_bc[:], in_=bias_h).then_inc(swBI, 16)
            g.dma_start(out=rb_bc[:], in_=rbb_h).then_inc(swRB, 16)

            def wload(rt):
                if rt >= 2:
                    g.wait_ge(spe, rt - 1)
                r0 = (rt % 4) * P
                c0 = (rt // 4) * 512
                g.dma_start(out=wtile[rt % 2][:], in_=wT[r0:r0 + P, c0:c0 + 512]).then_inc(spw[rt % 2], 16)

            wload(0)
            wload(1)
            for j in range(NCH):
                for ks in range(4):
                    rt = j * 4 + ks
                    if rt + 2 < KSL:
                        wload(rt + 2)
                g.wait_ge(spe, (j + 1) * 4)
                if DBG and j == 0:
                    g.dma_start(out=dbg_stage, in_=stage[:]).then_inc(sdbg, 16)
                    g.dma_start(out=dbg_wt, in_=wtile[1][:]).then_inc(sdbg, 16)
                    g.dma_start(out=dbg_bm[:, 0:16], in_=bmPs[:]).then_inc(sdbg, 16)
                    with nc.allow_non_contiguous_dma(reason="debug"):
                        g.dma_start(out=dbg_bm[:, 16:32], in_=rsb[:]).then_inc(sdbg, 16)
                        g.dma_start(out=dbg_bm[0:4, 32:48], in_=bm01[:]).then_inc(sdbg, 16)
                g.dma_start(out=ag_in[j].ap(), in_=stage[:]).then_inc(spd, 16)
                g.wait_ge(spd, (j + 1) * 16)
                g.collective_compute("AllGather", ALU.bypass, replica_groups=[list(range(NCORES))],
                                     ins=[ag_in[j].ap().opt()], outs=[ag_out[j].ap().opt()]).then_inc(sag, 1)

            g.wait_ge(swBI, 16)
            for c, ph in ((0, "L2A"), (1, "L2B")):
                gbase = LN_PH[ph] * 4
                for i in range(NTC):
                    t_abs = c * NTC + i
                    gidx = gbase + i
                    g.wait_ge(snm, (gidx + 1) * 8)
                    g.wait_ge(srb, gidx + 1)
                    g.tensor_tensor(xcol(t_abs, 0, D), xcol(t_abs, 0, D),
                                    bias_bc[:], ALU.add).then_inc(sbia, 1)

        @block.vector
        def _(v):
            def prep_piece(j):
                for ks in range(4):
                    rt = j * 4 + ks
                    j2 = rt % 2
                    v.wait_ge(spm[j2], 16 * (rt // 2 + 1))
                    v.wait_ge(spw[j2], 16 * (rt // 2 + 1))
                    if ks == 0 and j > 0:
                        v.wait_ge(spd, 16 * j)
                    stsl = stage[:, ks * 512:(ks + 1) * 512]
                    v.tensor_scalar(stsl, mtile[j2][:], 0.0, None, ALU.add)
                    with nc.allow_low_precision(reason="mask block-sums are small ints, exact in bf16"):
                        v.reduce_sum(rsb[:], stsl.rearrange("p (g o) -> p g o", o=32),
                                     axis=AX.X).then_inc(spsV, 1)
                    v.wait_ge(spsT, 2 * rt + 1)
                    v.tensor_scalar(bm01[:], p7[0:4, 0:16], 0.0, None, ALU.is_gt).then_inc(spsV, 1)
                    v.wait_ge(spsT, 2 * rt + 2)
                    v.tensor_copy(bmPs[:], p7[:, 0:16]).then_inc(spsV, 1)
                    v.tensor_copy(dmy[:], bmPs[:])
                    v.tensor_copy(dmy2[:], bmPs[:])
                    v.tensor_copy(dmy[:], bmPs[:])
                    v.tensor_tensor(
                        stage[:, ks * 512:(ks + 1) * 512].rearrange("p (g o) -> p g o", o=32),
                        wtile[j2][:].rearrange("p (g o) -> p g o", o=32),
                        bmPs[:].rearrange("p (g o) -> p g o", o=1).broadcast_to((P, 16, 32)),
                        ALU.mult).then_inc(spe, 1)

            def ln_dve(ph):
                gbase = LN_PH[ph] * 4
                c = LN_CHUNK[ph]
                xr = LN_XREADY[ph]
                sx = sxA if c == 0 else sxB
                for i in range(NTC):
                    gidx = gbase + i
                    t_abs = c * NTC + i
                    b2 = gidx % 2
                    v.wait_ge(sx, 16)
                    if xr is not None and i == 0:
                        for t in range(4):
                            v.wait_ge(sbank[t], xr)
                    if gidx >= 2:
                        v.wait_ge(snm, 8 * (gidx - 1))
                    for sl in range(8):
                        v.bn_stats(bnb[:, sl * 6:(sl + 1) * 6], xcol(t_abs, sl * 512, 512))
                    v.bn_aggr(magg[b2][:], bnb[:]).then_inc(svd, 1)
                    v.wait_ge(ssd, gidx + 1)
                    v.reciprocal(st_r[b2][:], st_sd[b2][:])
                    v.tensor_scalar(st_b[b2][:], magg[b2][:, 0:1], st_r[b2][:], -1.0,
                                    ALU.mult, ALU.mult)
                    v.tensor_copy(dmy[:], bmPs[:])
                    v.tensor_copy(dmy2[:], bmPs[:]).then_inc(srb, 1)

            def ffn_dve(ph):
                f = FFN_PH[ph]
                c = FFN_CHUNK[ph]
                for t in range(4):
                    idx = 4 * f + t
                    v.wait_ge(srld, idx + 1)
                    if f == 0 and t == 0:
                        v.wait_ge(swRB, 16)
                        v.wait_ge(swrmd, 16)
                    v.scalar_tensor_tensor(lg[t][:], p6[:, 0:8], 1.0, rb_bc[:],
                                           ALU.mult, ALU.add).then_inc(slgd, 1)
                for t in range(4):
                    v.reduce_max(g_m1[t][:], lg[t][:], axis=AX.X)
                for t in range(4):
                    v.tensor_scalar(f1a[t][:], lg[t][:], g_m1[t][:], None, ALU.is_equal)
                for t in range(4):
                    v.scalar_tensor_tensor(lg2[t][:], f1a[t][:], -1e30, lg[t][:],
                                           ALU.mult, ALU.add)
                for t in range(4):
                    v.reduce_max(g_m2[t][:], lg2[t][:], axis=AX.X)
                for t in range(4):
                    ins = v.tensor_tensor(g_d[t][:], g_m2[t][:], g_m1[t][:], ALU.subtract)
                    if t >= 2:
                        ins.then_inc(sgex, 1)
                v.tensor_copy(dmy[:], bmPs[:]).then_inc(sgex, 1)
                v.tensor_copy(dmy2[:], bmPs[:]).then_inc(sgex, 1)
                for t in range(4):
                    v.wait_ge(sgeb, 4 * f + t + 1)
                    v.tensor_scalar(g_den[t][:], g_e[t][:], 1.0, None, ALU.add)
                for t in range(4):
                    v.reciprocal(g_r[t][:], g_den[t][:])
                for t in range(4):
                    v.tensor_scalar(g_1[t][:], g_r[t][:], 0.25, None, ALU.mult)
                for t in range(4):
                    v.tensor_tensor(g_2[t][:], g_e[t][:], g_1[t][:], ALU.mult)
                for t in range(4):
                    v.tensor_scalar(f1b[t][:], lg2[t][:], g_m2[t][:], None, ALU.is_equal)
                if f >= 1:
                    v.wait_ge(swqt, 4 * f)
                for t in range(4):
                    v.tensor_scalar(wq[t][:], f1a[t][:], g_1[t][:], None, ALU.mult)
                for t in range(4):
                    ins = v.scalar_tensor_tensor(wq[t][:], f1b[t][:], g_2[t][:], wq[t][:],
                                                 ALU.mult, ALU.add)
                    if t >= 2:
                        ins.then_inc(swqd, 1)
                v.tensor_copy(dmy[:], bmPs[:]).then_inc(swqd, 1)
                v.tensor_copy(dmy2[:], bmPs[:]).then_inc(swqd, 1)
                v.wait_ge(swqt, 4 * f + 4)
                v.tensor_copy(w8[:], ptr[0][0:8, :])
                v.tensor_copy(dmy[:], bmPs[:])
                v.tensor_copy(dmy2[:], bmPs[:]).then_inc(sw8d, 1)
                v.wait_ge(swrd, f + 1)
                v.tensor_copy(wrs[:], p7[:, :]).then_inc(swre, 1)
                v.wait_ge(shd, f + 1)
                v.tensor_tensor(hw[:], p6[:, :], wrs[:], ALU.mult)
                v.tensor_copy(dmy[:], bmPs[:])
                v.tensor_copy(dmy2[:], bmPs[:]).then_inc(shwd, 1)
                for o in range(NCH):
                    for t in range(4):
                        t_abs = c * NTC + t
                        v.wait_ge(sm2s, 32 * f + o * 4 + t + 1)
                        v.scalar_tensor_tensor(xcol(t_abs, o * 512, 512), pa[t][:], 1.0,
                                               xcol(t_abs, o * 512, 512),
                                               ALU.mult, ALU.add).then_inc(sbank[t], 1)

            def attn_dve(c):
                base = ATTN_BANK_BASE[c]
                for nch in range(NCH):
                    for t in range(4):
                        t_abs = c * NTC + t
                        v.wait_ge(swf, c * 128 + nch * 16 + 16)
                        if nch == 0:
                            v.wait_ge(sbia, c * 4 + t + 1)
                        v.scalar_tensor_tensor(xcol(t_abs, nch * 512, 512), pa[t][:], 1.0,
                                               xcol(t_abs, nch * 512, 512),
                                               ALU.mult, ALU.add).then_inc(sbank[t], 1)

            prep_piece(0)
            prep_piece(1)
            ln_dve("L1A")
            prep_piece(2)
            prep_piece(3)
            ffn_dve("F1A")
            prep_piece(4)
            prep_piece(5)
            ln_dve("L1B")
            prep_piece(6)
            prep_piece(7)
            ffn_dve("F1B")
            ln_dve("L2A")
            attn_dve(0)
            ln_dve("L3A")
            ffn_dve("F2A")
            ln_dve("L2B")
            attn_dve(1)
            ln_dve("L3B")
            ffn_dve("F2B")

        @block.scalar
        def _(s):
            def ln_scalar(ph):
                gbase = LN_PH[ph] * 4
                c = LN_CHUNK[ph]
                sx = sxA if c == 0 else sxB
                xr = LN_XREADY[ph]
                for i in range(NTC):
                    gidx = gbase + i
                    t_abs = c * NTC + i
                    b2 = gidx % 2
                    s.wait_ge(sx, 16)
                    if i == 0 and xr is not None:
                        for t in range(4):
                            s.wait_ge(sbank[t], xr)
                    s.wait_ge(svd, gidx + 1)
                    s.activation(st_sd[b2][:], magg[b2][:, 1:2], AF.Sqrt,
                                 bias=epst[:], scale=1.0).then_inc(ssd, 1)
                    s.wait_ge(srb, gidx + 1)
                    for sl in range(8):
                        S_ = gidx * 8 + sl
                        if S_ >= 2:
                            s.wait_ge(strd, S_ - 1)
                        s.activation(xstg[S_ % 2][:], xcol(t_abs, sl * 512, 512), AF.Identity,
                                     bias=st_b[b2][:], scale=st_r[b2][:]).then_inc(snm, 1)
                        if DBG and ph == "L1A" and i == 0 and sl == 0:
                          with nc.allow_non_contiguous_dma(reason="debug taps"):
                            s.dma_start(out=dbg_xn, in_=xstg[S_ % 2][:]).then_inc(sdbg, 16)
                            s.dma_start(out=dbg_st[:, 0:2], in_=magg[b2][:]).then_inc(sdbg, 16)
                            s.dma_start(out=dbg_st[:, 11:12], in_=st_sd[b2][:]).then_inc(sdbg, 16)
                            s.dma_start(out=dbg_st[:, 12:13], in_=st_r[b2][:]).then_inc(sdbg, 16)
                            s.dma_start(out=dbg_st[:, 13:14], in_=st_b[b2][:]).then_inc(sdbg, 16)
                            s.dma_start(out=dbg_st[:, 14:15], in_=st_sd[b2][:]).then_inc(sdbg, 16)
                            s.dma_start(out=dbg_st[:, 15:16], in_=st_r[b2][:]).then_inc(sdbg, 16)
                            s.dma_start(out=dbg_st[:, 16:17], in_=st_b[b2][:]).then_inc(sdbg, 16)
                        if sl >= 1:
                            ev_batch(s, gidx * 8 + sl - 1)
                    ev_batch(s, gidx * 8 + 7)

            def ev_batch(s, B):
                gidx, sl = B // 8, B % 8
                ph = LN_ORDER[gidx // 4]
                i = gidx % 4
                s.wait_ge(strd, B + 1)
                dst = xnT[:].rearrange("p (kk tt) -> p kk tt", tt=512)[:, sl * 4:(sl + 1) * 4,
                                                                       i * P:(i + 1) * P]
                s.copy(dst, ptr[B % 2][:].rearrange("p (kk tt) -> p kk tt", tt=P)).then_inc(sted, 1)

            def ffn_scalar(ph):
                f = FFN_PH[ph]
                for t in range(4):
                    idx = 4 * f + t
                    s.wait_ge(sgex, idx + 1)
                    ins = s.activation(g_e[t][:], g_d[t][:], AF.Exp, bias=0.0, scale=1.0)
                    if t >= 2:
                        ins.then_inc(sgeb, 1)
                s.copy(sdmya[:], epst[:]).then_inc(sgeb, 1)
                s.copy(sdmya[:], epst[:]).then_inc(sgeb, 1)
                for t in range(4):
                    if DBG and f == 0 and t == 0:
                      with nc.allow_non_contiguous_dma(reason="debug taps"):
                        s.wait_ge(swqd, 1)
                        s.dma_start(out=dbg_gc, in_=gcap[:]).then_inc(sdbg, 16)
                        s.dma_start(out=dbg_lg[:, 0:8], in_=lg[0][:]).then_inc(sdbg, 16)

            ln_scalar("L1A")
            if DBG:
                s.wait_ge(sted, 32)
                s.dma_start(out=dbg_xnT, in_=xnT[:]).then_inc(sdbg, 16)
            ffn_scalar("F1A")
            ln_scalar("L1B")
            ffn_scalar("F1B")
            ln_scalar("L2A")
            ln_scalar("L3A")
            ffn_scalar("F2A")
            ln_scalar("L2B")
            ln_scalar("L3B")
            ffn_scalar("F2B")

        @block.tensor
        def _(t_):
            def prep_pe(j):
                for ks in range(4):
                    rt = j * 4 + ks
                    if rt == 16:
                        t_.wait_ge(swre, 1)
                    t_.wait_ge(spsV, 3 * rt + 1)
                    t_.matmul(p7[0:4, 0:16], ind32[:], rsb[:], start=True, stop=True).then_inc(spsT, 1)
                    t_.wait_ge(spsV, 3 * rt + 2)
                    t_.matmul(p7[0:128, 0:16], ind4[:], bm01[:], start=True, stop=True).then_inc(spsT, 1)

            def ln_pe(ph):
                gbase = LN_PH[ph] * 4
                LN_HWPRE2 = {"L1A": 0, "L1B": 1, "L2A": 2, "L3A": 2, "L2B": 3, "L3B": 3}
                if LN_HWPRE2[ph] > 0:
                    t_.wait_ge(sw8d, LN_HWPRE2[ph])
                for i in range(NTC):
                    gidx = gbase + i
                    for sl in range(8):
                        B = gidx * 8 + sl
                        t_.wait_ge(snm, B + 1)
                        if B >= 2:
                            t_.wait_ge(sted, B - 1)
                        for q in range(4):
                            ins = t_.transpose(ptr[B % 2][:, q * P:(q + 1) * P],
                                               xstg[B % 2][:, q * P:(q + 1) * P], ident[:])
                        ins.then_inc(strd, 1)

            def ffn_pe(ph):
                f = FFN_PH[ph]
                t_.wait_ge(sted, STED_END[{"F1A": "L1A", "F1B": "L1B", "F2A": "L3A", "F2B": "L3B"}[ph]])
                if f == 0:
                    t_.wait_ge(swRW, 16)
                    t_.wait_ge(swAT, 16)
                for t in range(4):
                    idx = 4 * f + t
                    if t == 0:
                        if f > 0:
                            t_.wait_ge(shwd, f)
                    else:
                        t_.wait_ge(slgd, idx)
                    for kk in range(KSL):
                        ins = t_.matmul(p6[:, 0:8],
                                        xnT[:, kk * 512 + t * P: kk * 512 + (t + 1) * P],
                                        rwT[:, kk * 8:(kk + 1) * 8],
                                        start=(kk == 0), stop=(kk == KSL - 1))
                    ins.then_inc(srld, 1)
                t_.wait_ge(sted, STED_END[{"F1A": "L1A", "F1B": "L1B", "F2A": "L3A", "F2B": "L3B"}[ph]])
                for t in range(4):
                    idx = 4 * f + t
                    t_.wait_ge(swqd, idx + 1)
                    t_.transpose(ptr[0][0:8, t * P:(t + 1) * P], wq[t][:], ident[:]).then_inc(swqt, 1)
                t_.wait_ge(slgd, 4 * f + 4)
                for kk in range(KSL):
                    ins = t_.matmul(p6[:, :], AT[:, kk * P:(kk + 1) * P],
                                    xnT[:, kk * 512:(kk + 1) * 512],
                                    start=(kk == 0), stop=(kk == KSL - 1))
                ins.then_inc(shd, 1)
                t_.wait_ge(sw8d, f + 1)
                if f == 0:
                    t_.wait_ge(spsV, 3 * 16)
                elif f == 1:
                    t_.wait_ge(spsV, 3 * 32)
                t_.matmul(p7[:, :], ind16[:], w8[:], start=True, stop=True).then_inc(swrd, 1)
                t_.wait_ge(shwd, f + 1)
                if f == 0:
                    t_.wait_ge(swBT, 16)
                base = FFN_BANK_BASE[ph]
                for o in range(NCH):
                    for t in range(4):
                        t_.wait_ge(sbank[t], base + o)
                        t_.matmul(pa[t][:], hw[:, t * P:(t + 1) * P], BT[:, o * 512:(o + 1) * 512],
                                  start=True, stop=True).then_inc(sm2s, 1)

            def attn_pe(c):
                base = ATTN_BANK_BASE[c]
                ln_src = "L2A" if c == 0 else "L2B"
                t_.wait_ge(sted, STED_END[ln_src])
                for nch in range(NCH):
                    for q in range(16):
                        Q = c * 128 + nch * 16 + q
                        t_.wait_ge(swq[Q % 2], 16 * (Q // 2 + 1))
                        for ki in range(2):
                            kk = q * 2 + ki
                            for t in range(4):
                                if kk == 0:
                                    t_.wait_ge(sbank[t], base + nch)
                                ins = t_.matmul(pa[t][:],
                                                xnT[:, kk * 512 + t * P: kk * 512 + (t + 1) * P],
                                                wqbuf[Q % 2][:, ki * 512:(ki + 1) * 512],
                                                start=(kk == 0), stop=(kk == KSL - 1))
                        ins.then_inc(swf, 1)

            t_.wait_ge(scon, 13)
            prep_pe(0)
            prep_pe(1)
            ln_pe("L1A")
            prep_pe(2)
            prep_pe(3)
            ffn_pe("F1A")
            prep_pe(4)
            prep_pe(5)
            ln_pe("L1B")
            prep_pe(6)
            prep_pe(7)
            ffn_pe("F1B")
            ln_pe("L2A")
            attn_pe(0)
            ln_pe("L3A")
            ffn_pe("F2A")
            ln_pe("L2B")
            attn_pe(1)
            ln_pe("L3B")
            ffn_pe("F2B")

    return nc


_NC_CACHE = None


def _get_nc():
    global _NC_CACHE
    if _NC_CACHE is None:
        _NC_CACHE = build_nc()
    return _NC_CACHE


def kernel(L0_attn_data, L0_attn_mask, L0_attn_weight, L0_attn_bias,
           L0_ffn_router_w, L0_ffn_router_b, L0_ffn_A, L0_ffn_Bm):
    x = np.ascontiguousarray(np.asarray(L0_attn_data, dtype=np.float32).reshape(8192, D))
    maskT = np.ascontiguousarray(np.asarray(L0_attn_mask, dtype=np.int32).T)
    wTf = np.ascontiguousarray(np.asarray(L0_attn_weight, dtype=np.float32).T)
    bias = np.asarray(L0_attn_bias, dtype=np.float32)
    bias_bc = np.ascontiguousarray(np.broadcast_to(bias[None, :], (P, D)))
    rw = np.asarray(L0_ffn_router_w, dtype=np.float32)
    rwT = np.ascontiguousarray(rw.T)
    rb = np.asarray(L0_ffn_router_b, dtype=np.float32)
    rb_bc = np.ascontiguousarray(np.broadcast_to(rb[None, :], (P, 8)))
    A = np.asarray(L0_ffn_A, dtype=np.float32).reshape(P, D)
    aT = np.ascontiguousarray(A.T)
    Bm = np.asarray(L0_ffn_Bm, dtype=np.float32)
    bT = np.ascontiguousarray(np.transpose(Bm, (0, 2, 1)).reshape(P, D))

    nc = _get_nc()
    in_maps = []
    for i in range(NCORES):
        in_maps.append(dict(
            xin=np.ascontiguousarray(x[i * TOK:(i + 1) * TOK]),
            maskT=np.ascontiguousarray(maskT[i * 512:(i + 1) * 512]),
            wT=np.ascontiguousarray(wTf[i * 512:(i + 1) * 512]),
            bias_h=bias_bc, rbb_h=rb_bc, rwT_h=rwT, aT_h=aT, bT_h=bT,
        ))
    res = run_bass_kernel_spmd(nc, in_maps, core_ids=list(range(NCORES)))
    out = np.concatenate([r["out_h"] for r in res.results], axis=0)
    return out.reshape(2, 4096, D)


if __name__ == "__main__":
    build_nc()
    print("build ok")
